# version 01 (final)
# baseline (speedup 1.0000x reference)
import numpy as np
import concourse.bass as bass
import concourse.mybir as mybir
from concourse.bass_utils import run_bass_kernel_spmd

F32 = mybir.dt.float32
BF16 = mybir.dt.bfloat16
AF = mybir.ActivationFunctionType
ALU = mybir.AluOpType
AX = mybir.AxisListType

MAGIC = 12582912.0
EPS = 1e-5
NCORES = 8
B, T, D, NCH = 2, 2048, 1024, 8
KV, NKVCH, NH, NKV, DH = 512, 4, 16, 8, 64
I_FF, NIC = 2816, 22
VW = 1088

ENGS = ("tensor", "vector", "scalar", "sync", "gpsimd")
SEMS = ("dG", "dS", "dA", "tT", "tV", "tC", "tG", "cc", "dXa", "dXb", "dY",
        "dW", "dKa", "dKb", "dWd", "dD2", "dOa", "dOb", "dH0", "dH1", "dH2",
        "dH3", "dYb")


class Prog:
    def __init__(self):
        self.ops = {e: [] for e in ENGS}
        self.cnt = {s: 0 for s in SEMS}
        self.sems = None

    def op(self, eng, make, inc=None, n=1):
        if inc is None:
            self.ops[eng].append(lambda e, mk=make: mk(e))
            return None
        self.cnt[inc] += n
        v = self.cnt[inc]
        self.ops[eng].append(
            lambda e, mk=make, s=inc, k=n: mk(e).then_inc(self.sems[s], k))
        return v

    def wait(self, eng, sem, val=None):
        v = self.cnt[sem] if val is None else val
        if v and v > 0:
            self.ops[eng].append(lambda e, s=sem, vv=v: e.wait_ge(self.sems[s], vv))


def build(dbg=()):
    nc = bass.Bass("TRN2", target_bir_lowering=False)
    dt = nc.dram_tensor

    x_d = dt("x", [B, T, D], F32, kind="ExternalInput").ap()
    wq_d = dt("wq", [D, D], F32, kind="ExternalInput").ap()
    wk_d = dt("wk", [D, KV], F32, kind="ExternalInput").ap()
    wv_d = dt("wv", [D, KV], F32, kind="ExternalInput").ap()
    mem_d = dt("mem", [KV, KV], F32, kind="ExternalInput").ap()
    mn_d = dt("mn", [1, KV], F32, kind="ExternalInput").ap()
    mw_d = dt("mw", [1, 1], F32, kind="ExternalInput").ap()
    clx_d = dt("clx", [1, 1], F32, kind="ExternalInput").ap()
    cld_d = dt("cld", [1, 1], F32, kind="ExternalInput").ap()
    wg_d = dt("wg", [I_FF, D], F32, kind="ExternalInput").ap()
    wu_d = dt("wu", [I_FF, D], F32, kind="ExternalInput").ap()
    wd_d = dt("wd", [D, I_FF], F32, kind="ExternalInput").ap()
    c_d = dt("cvec", [B, D], F32, kind="ExternalInput").ap()
    wmod_d = dt("wmod", [D, 6 * D], F32, kind="ExternalInput").ap()
    y_d = dt("y", [B, T, D], F32, kind="ExternalOutput").ap()
    x1_sc = dt("x1_sc", [B, T, D], F32, kind="Internal").ap()
    ln2_sc = dt("ln2_sc", [B, NCH, 128, T], BF16, kind="Internal").ap()
    hmid_sc = dt("hmid_sc", [NIC, 128, B * T], BF16, kind="Internal").ap()
    cc1i = dt("cc1i", [1, 1], F32, kind="Internal").ap()
    cc1o = dt("cc1o", [1, 1], F32, kind="Internal").ap()
    cc2i = dt("cc2i", [1, 1], F32, kind="Internal").ap()
    cc2o = dt("cc2o", [1, 1], F32, kind="Internal").ap()
    dbg_d = {}
    for name, shape in dbg:
        dbg_d[name] = dt("dbg_" + name, list(shape), F32,
                         kind="ExternalOutput").ap()

    p = Prog()
    USZ = 86016

    import contextlib
    with contextlib.ExitStack() as _es:
        capc = _es.enter_context(nc.sbuf_tensor("capc", [128, NCH * B], F32))
        identb = _es.enter_context(nc.sbuf_tensor("identb", [128, 128], BF16))
        identf = _es.enter_context(nc.sbuf_tensor("identf", [128, 128], F32))
        onesb = _es.enter_context(nc.sbuf_tensor("onesb", [128, 128], BF16))
        bdb = _es.enter_context(nc.sbuf_tensor("bdb", [128, 2], BF16))
        memsb = _es.enter_context(nc.sbuf_tensor("memsb", [128, NKVCH * KV], BF16))
        mncol = _es.enter_context(nc.sbuf_tensor("mncol", [128, NKVCH * 128], BF16))
        mnrow = _es.enter_context(nc.sbuf_tensor("mnrow", [1, KV], BF16))
        silc = _es.enter_context(nc.sbuf_tensor("silc", [B, D], BF16))
        silcT = _es.enter_context(nc.sbuf_tensor("silcT", [128, 2 * NCH], BF16))
        pcols = _es.enter_context(nc.sbuf_tensor("pcols", [128, 6 * NCH * B], F32))
        pcolsb = _es.enter_context(nc.sbuf_tensor("pcolsb", [128, 6 * NCH * B], BF16))
        cbpc = _es.enter_context(nc.sbuf_tensor("cbpc", [128, NCH * B], F32))
        g1wc = _es.enter_context(nc.sbuf_tensor("g1wc", [128, NCH * B], F32))
        gawc = _es.enter_context(nc.sbuf_tensor("gawc", [128, NCH * B], F32))
        g1wvbc = _es.enter_context(nc.sbuf_tensor("g1wvbc", [128, NCH * B], F32))
        gbc = _es.enter_context(nc.sbuf_tensor("gbc", [128, NCH * B], F32))
        qbias = _es.enter_context(nc.sbuf_tensor("qbias", [128, NCH * B], F32))
        kbias = _es.enter_context(nc.sbuf_tensor("kbias", [128, NKV * B], F32))
        vbc = _es.enter_context(nc.sbuf_tensor("vbc", [128, NKVCH * B], F32))
        vbdup = _es.enter_context(nc.sbuf_tensor("vbdup", [128, NCH * B], F32))
        negM = _es.enter_context(nc.sbuf_tensor("negM", [128, NH * B], F32))
        statm = _es.enter_context(nc.sbuf_tensor("statm", [128, 16], F32))
        stat2 = _es.enter_context(nc.sbuf_tensor("stat2", [128, 16], F32))
        rstdc = _es.enter_context(nc.sbuf_tensor("rstdc", [128, 16], F32))
        negmrc = _es.enter_context(nc.sbuf_tensor("negmrc", [128, 16], F32))
        qn2 = _es.enter_context(nc.sbuf_tensor("qn2", [2, 16], F32))
        kn2 = _es.enter_context(nc.sbuf_tensor("kn2", [2, 16], F32))
        qnrow = _es.enter_context(nc.sbuf_tensor("qnrow", [1, 16], F32))
        knexp = _es.enter_context(nc.sbuf_tensor("knexp", [1, 16], F32))
        mrow = _es.enter_context(nc.sbuf_tensor("mrow", [1, 16], F32))
        mrowb = _es.enter_context(nc.sbuf_tensor("mrowb", [1, 16], BF16))
        srow = _es.enter_context(nc.sbuf_tensor("srow", [1, 16], F32))
        wcol = _es.enter_context(nc.sbuf_tensor("wcol", [128, 1], F32))
        w1mcol = _es.enter_context(nc.sbuf_tensor("w1mcol", [128, 1], F32))
        rsxcol = _es.enter_context(nc.sbuf_tensor("rsxcol", [128, 1], F32))
        sxcol = _es.enter_context(nc.sbuf_tensor("sxcol", [128, 1], F32))
        rsdcol = _es.enter_context(nc.sbuf_tensor("rsdcol", [128, 1], F32))
        sdcol = _es.enter_context(nc.sbuf_tensor("sdcol", [128, 1], F32))
        gmxcol = _es.enter_context(nc.sbuf_tensor("gmxcol", [128, 1], F32))
        hmxcol = _es.enter_context(nc.sbuf_tensor("hmxcol", [128, 1], F32))
        rsgc = _es.enter_context(nc.sbuf_tensor("rsgc", [128, 2], F32))
        rsdwn = _es.enter_context(nc.sbuf_tensor("rsdwn", [128, NCH], F32))
        combo = _es.enter_context(nc.sbuf_tensor("combo", [128, NCH * B], F32))
        qcol = _es.enter_context(nc.sbuf_tensor("qcol", [128, 1], F32))
        dsnap = _es.enter_context(nc.sbuf_tensor("dsnap", [128, 64], F32))
        rsuall = _es.enter_context(nc.sbuf_tensor("rsuall", [128, NIC], F32))
        rsq = _es.enter_context(nc.sbuf_tensor("rsq", [128, NIC], F32))
        qcol2 = _es.enter_context(nc.sbuf_tensor("qcol2", [128, 1], F32))
        zcol = _es.enter_context(nc.sbuf_tensor("zcol", [128, 1], F32))
        U = _es.enter_context(nc.sbuf_tensor("U", [128, USZ], BF16))
        scrA = _es.enter_context(nc.sbuf_tensor("scrA", [128, 2048], F32))
        scrB = _es.enter_context(nc.sbuf_tensor("scrB", [128, 2048], F32))
        pa = _es.enter_context(nc.psum_tensor("pa", [128, 1024], F32))
        pb = _es.enter_context(nc.psum_tensor("pb", [128, 1024], F32))
        pc = _es.enter_context(nc.psum_tensor("pc", [128, 1024], F32))
        pd = _es.enter_context(nc.psum_tensor("pd", [128, 1024], F32))
        s_dG = _es.enter_context(nc.semaphore("dG"))
        s_dS = _es.enter_context(nc.semaphore("dS"))
        s_dA = _es.enter_context(nc.semaphore("dA"))
        s_tT = _es.enter_context(nc.semaphore("tT"))
        s_tV = _es.enter_context(nc.semaphore("tV"))
        s_tC = _es.enter_context(nc.semaphore("tC"))
        s_tG = _es.enter_context(nc.semaphore("tG"))
        s_cc = _es.enter_context(nc.semaphore("ccs"))
        s_dXa = _es.enter_context(nc.semaphore("dXa"))
        s_dXb = _es.enter_context(nc.semaphore("dXb"))
        s_dY = _es.enter_context(nc.semaphore("dY"))
        s_dW = _es.enter_context(nc.semaphore("dW"))
        s_dKa = _es.enter_context(nc.semaphore("dKa"))
        s_dKb = _es.enter_context(nc.semaphore("dKb"))
        s_dWd = _es.enter_context(nc.semaphore("dWd"))
        s_dD2 = _es.enter_context(nc.semaphore("dD2"))
        s_dOa = _es.enter_context(nc.semaphore("dOa"))
        s_dOb = _es.enter_context(nc.semaphore("dOb"))
        s_dH0 = _es.enter_context(nc.semaphore("dH0"))
        s_dH1 = _es.enter_context(nc.semaphore("dH1"))
        s_dH2 = _es.enter_context(nc.semaphore("dH2"))
        s_dH3 = _es.enter_context(nc.semaphore("dH3"))
        s_dYb = _es.enter_context(nc.semaphore("dYb"))
        p.sems = {"dG": s_dG, "dS": s_dS, "dA": s_dA, "tT": s_tT, "tV": s_tV,
                  "tC": s_tC, "tG": s_tG, "cc": s_cc, "dXa": s_dXa,
                  "dXb": s_dXb, "dY": s_dY, "dW": s_dW, "dKa": s_dKa,
                  "dKb": s_dKb, "dWd": s_dWd, "dD2": s_dD2, "dOa": s_dOa,
                  "dOb": s_dOb, "dH0": s_dH0, "dH1": s_dH1, "dH2": s_dH2,
                  "dH3": s_dH3, "dYb": s_dYb}
        PS = [pa, pb, pc, pd]
        Uf = U.ap().bitcast(F32)
        paB = pa.ap().bitcast(BF16)
        pbB = pb.ap().bitcast(BF16)

        def ub(off, n):
            return U[:, off:off + n]

        def uf(off_b, n_f32):
            assert off_b % 2 == 0
            return Uf[:, off_b // 2: off_b // 2 + n_f32]

        dbg_done = set()

        def recip_lp(e, out, in_):
            with nc.allow_low_precision(reason="1/den feeds a bf16 matmul"):
                return e.reciprocal(out, in_)

        def dump(name, ap_fn):
            if name not in dbg_d or name in dbg_done:
                return
            dbg_done.add(name)
            for s in ("tT", "tV", "tC", "tG", "dG", "dS", "dA"):
                p.wait("gpsimd", s)
            p.op("gpsimd", lambda e, a=ap_fn, d=dbg_d[name]:
                 e.dma_start(d[:, :], a()), inc="dG", n=16)

        O_WMOD = 0
        O_MODF = 49152
        O_MODB = 61440
        O_LN1TM, O_HBUF = 0, 49152
        O_WQ, O_WK, O_WV = 65536, 73728, 81920
        O_QSB, O_KSB, O_VSB = 0, 16384, 32768
        O_SIG, O_PA, O_PA2, O_PB, O_PB2 = 50176, 54272, 55296, 56320, 57344
        O_RROW, O_S1, O_SQS = 58368, 59392, 60416
        O_DELTA = 65536
        O_D1TM, O_LN2TM, O_LN2FM = 0, 16384, 32768
        O_KX, O_KWT, O_RAW, O_KWNAT = 0, 32768, 36864, 45056
        O_HMIDB, O_KWDT, O_WDRAW, O_WDNAT = 47104, 51200, 73728, 79360
        O_LN2LD = 36864
        O_KH, O_D2B, O_D2TM, O_OUTST, O_X1LD = 0, 22528, 26624, 30720, 34816
        O_HMLD = 38912

        p.op("gpsimd", lambda e: e.memset(identb[:, :], 0.0))
        p.op("gpsimd", lambda e: e.affine_select(
            out=identb[:, :], in_=identb[:, :], compare_op=ALU.not_equal,
            fill=1.0, base=0, pattern=[[-1, 128]], channel_multiplier=1))
        p.op("gpsimd", lambda e: e.memset(identf[:, :], 0.0))
        p.op("gpsimd", lambda e: e.affine_select(
            out=identf[:, :], in_=identf[:, :], compare_op=ALU.not_equal,
            fill=1.0, base=0, pattern=[[-1, 128]], channel_multiplier=1))
        p.op("gpsimd", lambda e: e.memset(onesb[:, :], 1.0))
        p.op("gpsimd", lambda e: e.memset(bdb[:, :], 0.0))
        p.op("gpsimd", lambda e: e.memset(bdb[0:64, 0:1], 1.0))
        p.op("gpsimd", lambda e: e.memset(bdb[64:128, 1:2], 1.0))
        p.op("gpsimd", lambda e: e.memset(gmxcol[:, :], 0.0))
        p.op("gpsimd", lambda e: e.memset(zcol[:, :], 0.0))
        p.op("gpsimd", lambda e: e.memset(hmxcol[:, :], 0.0), inc="tG")
        init_done = p.cnt["tG"]

        p.op("gpsimd", lambda e: e.dma_start(mnrow[:, :], mn_d[:, :]),
             inc="dG", n=16)
        p.op("gpsimd", lambda e: e.dma_start(srow[:1, 12:13], mw_d[:, :]),
             inc="dG", n=16)
        p.op("gpsimd", lambda e: e.dma_start(srow[:1, 10:11], clx_d[:, :]),
             inc="dG", n=16)
        p.op("gpsimd", lambda e: e.dma_start(srow[:1, 11:12], cld_d[:, :]),
             inc="dG", n=16)
        p.op("gpsimd", lambda e: e.dma_start(silc[:, :], c_d[:, :]),
             inc="dG", n=16)
        for kc in range(NKVCH):
            p.op("gpsimd", lambda e, kc=kc: e.dma_start(
                memsb[:, kc * KV:(kc + 1) * KV],
                mem_d[kc * 128:(kc + 1) * 128, :]), inc="dG", n=16)
        for kc in range(NCH):
            p.op("gpsimd", lambda e, kc=kc: e.dma_start(
                ub(O_WMOD + kc * 6144, 6144),
                wmod_d[kc * 128:(kc + 1) * 128, :]), inc="dG", n=16)
        ld0 = p.cnt["dG"]
        p.wait("gpsimd", "dG", ld0)

        p.wait("scalar", "dG", ld0)
        p.op("scalar", lambda e: e.activation(silc[:, :], silc[:, :], AF.Silu),
             inc="tC")
        p.op("scalar", lambda e: e.activation(srow[:1, 0:1], srow[:1, 12:13],
                                              AF.Sigmoid), inc="tC")
        sil_done = p.cnt["tC"]
        p.wait("vector", "tC", sil_done)
        p.op("vector", lambda e: e.tensor_scalar(srow[:1, 1:2], srow[:1, 0:1],
                                                 -1.0, 1.0, ALU.mult, ALU.add),
             inc="tV")

        p.wait("tensor", "tC", sil_done)
        p.wait("tensor", "tG", init_done)
        for kc in range(NCH):
            p.op("tensor", lambda e, kc=kc: e.transpose(
                paB[:, kc * 2:kc * 2 + 2], silc[:, kc * 128:(kc + 1) * 128],
                identb[:2, :2]), inc=("tT" if kc == NCH - 1 else None))
        v = p.cnt["tT"]
        p.wait("vector", "tT", v)
        p.op("vector", lambda e: e.tensor_copy(silcT[:, :], paB[:, :2 * NCH]),
             inc="tV")
        silcT_done = p.cnt["tV"]

        for kc in range(NKVCH):
            p.op("tensor", lambda e, kc=kc: e.matmul(
                pb[:, kc * 128:(kc + 1) * 128],
                mnrow[:1, kc * 128:(kc + 1) * 128], onesb[:1, :128],
                start=True, stop=True), inc=("tT" if kc == NKVCH - 1 else None))
        v = p.cnt["tT"]
        p.wait("vector", "tT", v)
        p.op("vector", lambda e: e.tensor_copy(mncol[:, :], pb[:, :NKVCH * 128]),
             inc="tV")
        mncol_done = p.cnt["tV"]

        p.wait("tensor", "tV", silcT_done)
        mm_evac = []
        for ns in range(12):
            ps = PS[2 + (ns % 2)]
            if len(mm_evac) >= 2:
                p.wait("tensor", "tC", mm_evac[-2])
            for kc in range(NCH):
                p.op("tensor", lambda e, ns=ns, kc=kc, ps=ps: e.matmul(
                    ps[:2, :512], silcT[:, kc * 2:kc * 2 + 2],
                    ub(O_WMOD + kc * 6144 + ns * 512, 512), start=(kc == 0),
                    stop=(kc == NCH - 1)),
                    inc=("tT" if kc == NCH - 1 else None))
            v = p.cnt["tT"]
            p.wait("scalar", "tT", v)
            mm_evac.append(p.op("scalar", lambda e, ns=ns, ps=ps: e.copy(
                uf(O_MODF, 6144)[:2, ns * 512:(ns + 1) * 512], ps[:2, :512]),
                inc="tC"))
        mod_done = p.cnt["tC"]
        p.wait("vector", "tC", mod_done)
        p.op("vector", lambda e: e.tensor_copy(ub(O_MODB, 6144)[:2, :],
                                               uf(O_MODF, 6144)[:2, :]),
             inc="tV")
        modbf_done = p.cnt["tV"]

        p.wait("tensor", "tV", modbf_done)
        for pr in range(6):
            for cc in range(NCH):
                p.op("tensor", lambda e, pr=pr, cc=cc: e.transpose(
                    paB[:, (pr * NCH + cc) * 2:(pr * NCH + cc) * 2 + 2],
                    ub(O_MODB, 6144)[:2, pr * D + cc * 128:
                                     pr * D + (cc + 1) * 128],
                    identb[:2, :2]),
                    inc=("tT" if (pr == 5 and cc == NCH - 1) else None))
        v = p.cnt["tT"]
        p.wait("vector", "tT", v)
        p.op("vector", lambda e: e.tensor_copy(pcols[:, :], paB[:, :96]),
             inc="tV")
        p.op("vector", lambda e: e.tensor_copy(pcolsb[:, :], paB[:, :96]),
             inc="tV")
        p.wait("vector", "tV")
        pcols_done = p.cnt["tV"]

        p.wait("vector", "tV", pcols_done)
        p.op("vector", lambda e: e.tensor_copy(mrowb[:1, 0:2], srow[:1, 0:2]),
             inc="tV")
        v = p.cnt["tV"]
        p.wait("tensor", "tV", v)
        p.op("tensor", lambda e: e.matmul(pb[:, 0:1], onesb[:1, :128],
                                          mrowb[:1, 0:1], start=True,
                                          stop=True), inc="tT")
        p.op("tensor", lambda e: e.matmul(pb[:, 1:2], onesb[:1, :128],
                                          mrowb[:1, 1:2], start=True,
                                          stop=True), inc="tT")
        v = p.cnt["tT"]
        p.wait("vector", "tT", v)
        p.op("vector", lambda e: e.tensor_copy(wcol[:, :], pb[:, 0:1]),
             inc="tV")
        p.op("vector", lambda e: e.tensor_copy(w1mcol[:, :], pb[:, 1:2]),
             inc="tV")
        p.wait("vector", "tV")

        def pslice(pr):
            return pcols[:, pr * NCH * B:(pr + 1) * NCH * B]

        p.op("vector", lambda e: e.tensor_scalar(capc[:, :], pslice(1), 1.0,
                                                 None, ALU.add), inc="tV")
        p.op("vector", lambda e: e.tensor_scalar(cbpc[:, :], pslice(4), 1.0,
                                                 None, ALU.add), inc="tV")
        p.op("vector", lambda e: e.tensor_scalar(scrB[:, 16:32], pslice(2),
                                                 1e-30, None, ALU.add),
             inc="tV")
        p.wait("vector", "tV")
        p.op("vector", lambda e: e.tensor_scalar(scrB[:, 48:64], pslice(5),
                                                 1e-30, None, ALU.add),
             inc="tV")
        p.wait("vector", "tV")
        sgv = p.cnt["tV"]
        p.wait("scalar", "tV", sgv)
        p.op("scalar", lambda e: e.activation(scrB[:, 16:32], scrB[:, 16:32],
                                              AF.Sign), inc="tC")
        p.op("scalar", lambda e: e.activation(scrB[:, 48:64], scrB[:, 48:64],
                                              AF.Sign), inc="tC")
        sgc = p.cnt["tC"]
        p.op("vector", lambda e: e.tensor_scalar(scrA[:, 0:16], pslice(2),
                                                 -1.0, None, ALU.mult),
             inc="tV")
        p.wait("vector", "tV")
        p.op("vector", lambda e: e.tensor_tensor(scrA[:, 0:16], pslice(2),
                                                 scrA[:, 0:16], ALU.max),
             inc="tV")
        p.wait("vector", "tV")
        p.op("vector", lambda e: e.tensor_scalar(scrA[:, 0:16], scrA[:, 0:16],
                                                 0.25, None, ALU.max),
             inc="tV")
        p.wait("vector", "tV")
        p.wait("vector", "tC", sgc)
        p.op("vector", lambda e: e.tensor_tensor(scrA[:, 0:16], scrA[:, 0:16],
                                                 scrB[:, 16:32], ALU.mult),
             inc="tV")
        p.wait("vector", "tV")
        p.op("vector", lambda e: e.tensor_scalar(g1wc[:, :], scrA[:, 0:16],
                                                 w1mcol[:, 0:1], None,
                                                 ALU.mult), inc="tV")
        p.wait("vector", "tV")
        p.op("vector", lambda e: e.tensor_scalar(gawc[:, :], scrA[:, 0:16],
                                                 wcol[:, 0:1], None, ALU.mult),
             inc="tV")
        p.wait("vector", "tV")
        p.op("vector", lambda e: e.tensor_scalar(scrA[:, 32:48], pslice(5),
                                                 -1.0, None, ALU.mult),
             inc="tV")
        p.wait("vector", "tV")
        p.op("vector", lambda e: e.tensor_tensor(scrA[:, 32:48], pslice(5),
                                                 scrA[:, 32:48], ALU.max),
             inc="tV")
        p.wait("vector", "tV")
        p.op("vector", lambda e: e.tensor_scalar(scrA[:, 32:48],
                                                 scrA[:, 32:48], 0.25, None,
                                                 ALU.max), inc="tV")
        p.wait("vector", "tV")
        p.op("vector", lambda e: e.tensor_tensor(gbc[:, :], scrA[:, 32:48],
                                                 scrB[:, 48:64], ALU.mult),
             inc="tV")
        params_done = p.cnt["tV"]
        dump("mod", lambda: uf(O_MODF, 6144)[:2, :])
        dump("capc", lambda: capc[:, :])
        dump("pcols", lambda: pcols[:, :])

        prev_ln2fm_done = 0
        for b in range(B):
            p.wait("gpsimd", "tV", prev_ln2fm_done)
            p.wait("gpsimd", "tV", params_done)
            p.wait("gpsimd", "dG")
            p.op("gpsimd", lambda e: e.memset(ub(O_VSB, 16 * VW), 1.0),
                 inc="tG")
            vones_done = p.cnt["tG"]
            half_rd = [0, 0]
            half_sq = [0, 0]
            p.wait("gpsimd", "tV", params_done)
            for tt in range(16):
                half = tt % 2
                sx = "dXa" if half == 0 else "dXb"
                p.wait("gpsimd", "tV", half_rd[half])
                ld = p.op("gpsimd", lambda e, b=b, tt=tt, half=half:
                          e.dma_start(scrA[:, half * 1024:(half + 1) * 1024],
                                      x_d[b, tt * 128:(tt + 1) * 128, :]),
                          inc=sx, n=16)
                p.wait("vector", sx, ld)
                p.op("vector", lambda e, tt=tt, half=half: e.tensor_reduce(
                    statm[:, tt:tt + 1],
                    scrA[:, half * 1024:(half + 1) * 1024], AX.X, ALU.add),
                    inc="tV")
                p.wait("scalar", sx, ld)
                p.wait("scalar", "tV", half_sq[half])
                sq = p.op("scalar", lambda e, half=half: e.activation(
                    scrB[:, half * 1024:(half + 1) * 1024],
                    scrA[:, half * 1024:(half + 1) * 1024], AF.Square),
                    inc="tC")
                p.wait("vector", "tC", sq)
                p.op("vector", lambda e, tt=tt, half=half: e.tensor_reduce(
                    stat2[:, tt:tt + 1],
                    scrB[:, half * 1024:(half + 1) * 1024], AX.X, ALU.add),
                    inc="tV")
                half_sq[half] = p.cnt["tV"]
                half_rd[half] = p.cnt["tV"]
            stats_done = p.cnt["tV"]
            if b == 0:
                p.op("vector", lambda e: e.tensor_copy(dsnap[:, 0:16],
                                                       statm[:, :]), inc="tV")
                p.op("vector", lambda e: e.tensor_copy(dsnap[:, 16:32],
                                                       stat2[:, :]), inc="tV")
            dump("snap", lambda: dsnap[:, :])
            p.wait("scalar", "tV", stats_done)
            bar = p.op("scalar", lambda e: e.copy(scrB[:, 100:101],
                                                  statm[:, 0:1]), inc="tC")
            p.wait("vector", "tC", bar)
            p.op("vector", lambda e: e.tensor_scalar(scrB[:, 0:16],
                                                     statm[:, :], 1.0 / D,
                                                     None, ALU.mult), inc="tV")
            p.wait("vector", "tV")
            p.op("vector", lambda e: e.tensor_tensor(scrB[:, 16:32],
                                                     scrB[:, 0:16],
                                                     scrB[:, 0:16], ALU.mult),
                 inc="tV")
            p.wait("vector", "tV")
            p.op("vector", lambda e: e.scalar_tensor_tensor(
                scrB[:, 32:48], stat2[:, :], 1.0 / D, scrB[:, 16:32],
                ALU.mult, ALU.subtract), inc="tV")
            p.wait("vector", "tV")
            p.op("vector", lambda e: e.tensor_scalar(scrB[:, 48:64],
                                                     scrB[:, 32:48], EPS,
                                                     None, ALU.add), inc="tV")
            p.wait("vector", "tV")
            p.op("vector", lambda e: e.reciprocal(scrB[:, 64:80],
                                                  scrB[:, 48:64]), inc="tV")
            v = p.cnt["tV"]
            p.wait("scalar", "tV", v)
            p.op("scalar", lambda e: e.activation(rstdc[:, :], scrB[:, 64:80],
                                                  AF.Sqrt), inc="tC")
            v = p.cnt["tC"]
            p.wait("vector", "tC", v)
            p.op("vector", lambda e: e.scalar_tensor_tensor(
                negmrc[:, :], scrB[:, 0:16], -1.0, rstdc[:, :], ALU.mult,
                ALU.mult), inc="tV")
            rstd_done = p.cnt["tV"]
            dump("rstd", lambda: rstdc[:, :])
            dump("stat2", lambda: stat2[:, :])
            dump("statm", lambda: statm[:, :])

            p.wait("scalar", "tV", rstd_done)
            half_ap = [0, 0]
            p.wait("gpsimd", "tV", rstd_done)
            p.wait("gpsimd", "tC")
            for tt in range(16):
                half = tt % 2
                sx = "dXa" if half == 0 else "dXb"
                p.wait("gpsimd", "tC", half_ap[half])
                ld = p.op("gpsimd", lambda e, b=b, tt=tt, half=half:
                          e.dma_start(scrA[:, half * 1024:(half + 1) * 1024],
                                      x_d[b, tt * 128:(tt + 1) * 128, :]),
                          inc=sx, n=16)
                p.wait("scalar", sx, ld)
                half_ap[half] = p.op("scalar", lambda e, tt=tt, half=half:
                                     e.activation(
                                         ub(O_LN1TM + tt * 1024, 1024),
                                         scrA[:, half * 1024:(half + 1) * 1024],
                                         AF.Identity, bias=negmrc[:, tt:tt + 1],
                                         scale=rstdc[:, tt:tt + 1]), inc="tC")
            ln1_done = p.cnt["tC"]
            dump("ln1", lambda: ub(O_LN1TM, 16384))

            for kc in range(NCH):
                p.op("gpsimd", lambda e, kc=kc: e.dma_start(
                    ub(O_WQ + kc * 1024, 1024),
                    wq_d[kc * 128:(kc + 1) * 128, :]), inc="dG", n=16)
            for kc in range(NCH):
                for c in range(NKV):
                    for hh in range(2):
                        p.op("gpsimd", lambda e, kc=kc, c=c, hh=hh: e.dma_start(
                            ub(O_WK + kc * 1024 + c * 128 + hh * 64, 64),
                            wk_d[kc * 128:(kc + 1) * 128,
                                 c * 64:(c + 1) * 64]), inc="dG", n=16)
            for kc in range(NCH):
                p.op("gpsimd", lambda e, kc=kc: e.dma_start(
                    ub(O_WV + kc * 512, 512),
                    wv_d[kc * 128:(kc + 1) * 128, :]), inc="dG", n=16)
            wload_done = p.cnt["dG"]
            p.wait("gpsimd", "dG", wload_done)

            p.wait("tensor", "tC", ln1_done)
            p.wait("tensor", "tV", params_done)
            ev_prev = 0
            for cc in range(NCH):
                view = paB if cc % 2 == 0 else pbB
                if cc >= 2:
                    p.wait("tensor", "tC", ev_prev)
                for tt in range(16):
                    p.op("tensor", lambda e, cc=cc, tt=tt, view=view:
                         e.transpose(view[:, tt * 128:(tt + 1) * 128],
                                     ub(O_LN1TM + tt * 1024 + cc * 128, 128),
                                     identb[:, :]),
                         inc=("tT" if tt == 15 else None))
                v = p.cnt["tT"]
                p.wait("scalar", "tT", v)
                ev_prev = p.op("scalar", lambda e, b=b, cc=cc, view=view:
                               e.activation(
                                   ub(O_HBUF + cc * T, T), view[:, :T],
                                   AF.Identity,
                                   scale=capc[:, cc * 2 + b:cc * 2 + b + 1]),
                               inc="tC")
            h_done = p.cnt["tC"]
            p.wait("tensor", "tC", h_done)
            dump("h", lambda: ub(O_HBUF, 16384))
            dump("capclate", lambda: capc[:, :])

            p.wait("tensor", "dG", wload_done)

            def bias_mms(woff, nfc, wstride, outcols, b=b):
                for fc in range(nfc):
                    for kc in range(NCH):
                        p.op("tensor", lambda e, fc=fc, kc=kc, woff=woff,
                             wstride=wstride, b=b: e.matmul(
                                 pc[:, 0:1],
                                 ub(woff + kc * wstride + fc * 128, 128),
                                 pcolsb[:, kc * 2 + b:kc * 2 + b + 1],
                                 start=(kc == 0), stop=(kc == NCH - 1)),
                             inc=("tT" if kc == NCH - 1 else None))
                    v = p.cnt["tT"]
                    p.wait("vector", "tT", v)
                    ev = p.op("vector", lambda e, fc=fc, outcols=outcols, b=b:
                              e.tensor_copy(outcols[:, fc * 2 + b:
                                            fc * 2 + b + 1], pc[:, 0:1]),
                              inc="tV")
                    p.wait("tensor", "tV", ev)

            bias_mms(O_WQ, NCH, 1024, qbias)
            bias_mms(O_WK, NKV, 1024, kbias)
            bias_mms(O_WV, NKVCH, 512, vbc)
            bias_done = p.cnt["tV"]

            p.wait("gpsimd", "tV", bias_done)
            for cc in range(NCH):
                vc, hh = cc // 2, cc % 2
                for dst in range(2):
                    p.op("gpsimd", lambda e, cc=cc, vc=vc, hh=hh, dst=dst, b=b:
                         e.dma_start(
                             vbdup[dst * 64:dst * 64 + 64,
                                   cc * 2 + b:cc * 2 + b + 1],
                             vbc[hh * 64:hh * 64 + 64,
                                 vc * 2 + b:vc * 2 + b + 1]), inc="dG", n=16)
            vbd_done = p.cnt["dG"]
            p.wait("gpsimd", "dG", vbd_done)
            p.wait("vector", "dG", vbd_done)
            p.op("vector", lambda e: e.tensor_tensor(g1wvbc[:, :], g1wc[:, :],
                                                     vbdup[:, :], ALU.mult),
                 inc="tV")
            p.wait("tensor", "tV", p.cnt["tV"])

            qt = [(pa, 0), (pa, 512), (pb, 0), (pb, 512)]
            qt_last = [0, 0, 0, 0]
            ridx = [0]

            def proj_tiles(n_out_tiles, mk_mm, mk_evac):
                for i in range(n_out_tiles):
                    r = ridx[0] % 4
                    ridx[0] += 1
                    ps, co = qt[r]
                    p.wait("tensor", "tC", qt_last[r])
                    for kc in range(NCH):
                        p.op("tensor", lambda e, i=i, kc=kc, ps=ps, co=co:
                             mk_mm(e, i, kc, ps, co),
                             inc=("tT" if kc == NCH - 1 else None))
                    v = p.cnt["tT"]
                    p.wait("scalar", "tT", v)
                    qt_last[r] = p.op("scalar", lambda e, i=i, ps=ps, co=co:
                                      mk_evac(e, i, ps, co), inc="tC")

            def q_mm(e, i, kc, ps, co, b=b):
                fc, ts = i // 4, i % 4
                return e.matmul(ps[:, co:co + 512],
                                ub(O_WQ + kc * 1024 + fc * 128, 128),
                                ub(O_HBUF + kc * T + ts * 512, 512),
                                start=(kc == 0), stop=(kc == NCH - 1))

            def q_ev(e, i, ps, co, b=b):
                fc, ts = i // 4, i % 4
                return e.activation(ub(O_QSB + fc * T + ts * 512, 512),
                                    ps[:, co:co + 512], AF.Identity,
                                    bias=qbias[:, fc * 2 + b:fc * 2 + b + 1])

            def k_mm(e, i, kc, ps, co, b=b):
                fc, ts = i // 4, i % 4
                return e.matmul(ps[:, co:co + 512],
                                ub(O_WK + kc * 1024 + fc * 128, 128),
                                ub(O_HBUF + kc * T + ts * 512, 512),
                                start=(kc == 0), stop=(kc == NCH - 1))

            def k_ev(e, i, ps, co, b=b):
                fc, ts = i // 4, i % 4
                return e.activation(ub(O_KSB + fc * T + ts * 512, 512),
                                    ps[:, co:co + 512], AF.Identity,
                                    bias=kbias[:, fc * 2 + b:fc * 2 + b + 1])

            def v_mm(e, i, kc, ps, co, b=b):
                tt = i
                return e.matmul(ps[:, co:co + 512],
                                ub(O_HBUF + kc * T + tt * 128, 128),
                                ub(O_WV + kc * 512, 512),
                                start=(kc == 0), stop=(kc == NCH - 1))

            def v_ev(e, i, ps, co, b=b):
                tt = i
                return e.copy(
                    ub(O_VSB + tt * VW + 64, 1024).rearrange(
                        "p (c d) -> p c d", d=128)[:, :, 0:64],
                    ps[:, co:co + 512].rearrange("p (c d) -> p c d", d=64))

            p.wait("scalar", "tG", vones_done)
            proj_tiles(32, q_mm, q_ev)
            proj_tiles(32, k_mm, k_ev)
            proj_tiles(16, v_mm, v_ev)
            qkv_done = p.cnt["tC"]
            dump("q", lambda: ub(O_QSB, 16384))
            dump("k", lambda: ub(O_KSB, 16384))

            p.wait("tensor", "tC", qkv_done)
            for src_off, dst in ((O_QSB, qn2), (O_KSB, kn2)):
                for cc in range(NCH):
                    sqv = p.op("scalar", lambda e, src_off=src_off, cc=cc:
                               e.activation(ub(O_SQS, T),
                                            ub(src_off + cc * T, T),
                                            AF.Square), inc="tC")
                    p.wait("tensor", "tC", sqv)
                    for j in range(4):
                        ps = [pc, pc, pd, pd][j]
                        co = (j % 2) * 512
                        p.op("tensor", lambda e, j=j, ps=ps, co=co: e.matmul(
                            ps[:2, co:co + 512], bdb[:, :],
                            ub(O_SQS + j * 512, 512), start=True, stop=True),
                            inc=("tT" if j == 3 else None))
                    v = p.cnt["tT"]
                    p.wait("vector", "tT", v)
                    p.op("vector", lambda e: e.tensor_reduce(
                        scrA[:2, 0:1], pc[:2, :1024], AX.X, ALU.max), inc="tV")
                    p.op("vector", lambda e: e.tensor_reduce(
                        scrA[:2, 1:2], pd[:2, :1024], AX.X, ALU.max), inc="tV")
                    p.wait("vector", "tV")
                    ev = p.op("vector", lambda e, dst=dst, cc=cc, b=b:
                              e.tensor_reduce(
                                  dst[:2, b * 8 + cc:b * 8 + cc + 1],
                                  scrA[:2, 0:2], AX.X, ALU.max), inc="tV")
                    p.wait("tensor", "tV", ev)
                    p.wait("scalar", "tV", ev)
            norm_done = p.cnt["tV"]
            p.wait("gpsimd", "tV", norm_done)
            for (src, dstrow, dup) in ((qn2, qnrow, False), (kn2, knexp, True)):
                for pr in range(2):
                    p.op("gpsimd", lambda e, src=src, dstrow=dstrow, pr=pr,
                         b=b, dup=dup: e.dma_start(
                             dstrow[0:1, :].rearrange("p (a c) -> p a c", c=2)
                             [:, :, pr:pr + 1],
                             src[0 if dup else pr:(0 if dup else pr) + 1,
                                 b * 8:b * 8 + 8],
                             allow_slow_non_contiguous=True), inc="dG", n=16)
            asm_done = p.cnt["dG"]
            p.wait("gpsimd", "dG", asm_done)
            p.wait("scalar", "dG", asm_done)
            p.op("scalar", lambda e: e.activation(qnrow[:1, :], qnrow[:1, :],
                                                  AF.Sqrt), inc="tC")
            p.op("scalar", lambda e: e.activation(knexp[:1, :], knexp[:1, :],
                                                  AF.Sqrt), inc="tC")
            v = p.cnt["tC"]
            p.wait("vector", "tC", v)
            p.op("vector", lambda e: e.tensor_tensor(mrow[:1, :], qnrow[:1, :],
                                                     knexp[:1, :], ALU.mult),
                 inc="tV")
            p.wait("vector", "tV")
            p.op("vector", lambda e: e.tensor_scalar(mrow[:1, :], mrow[:1, :],
                                                     -0.125, None, ALU.mult),
                 inc="tV")
            p.wait("vector", "tV")
            p.op("vector", lambda e: e.tensor_copy(mrowb[:1, :], mrow[:1, :]),
                 inc="tV")
            v = p.cnt["tV"]
            p.wait("tensor", "tV", v)
            p.op("tensor", lambda e: e.matmul(pc[:, 0:16], onesb[:1, :128],
                                              mrowb[:1, :16], start=True,
                                              stop=True), inc="tT")
            v = p.cnt["tT"]
            p.wait("vector", "tT", v)
            nm = p.op("vector", lambda e, b=b: e.tensor_copy(
                negM[:, b * 16:(b + 1) * 16], pc[:, 0:16]), inc="tV")
            p.wait("tensor", "tV", nm)
            p.wait("scalar", "tV", nm)

            for g in range(2):
                for th in range(2):
                    for kc in range(NKVCH):
                        qoff = O_QSB + (g * 4 + kc) * T + th * 1024
                        soff = O_SIG + kc * 1024
                        p.op("vector", lambda e, qoff=qoff, soff=soff:
                             e.tensor_scalar(ub(soff, 1024), ub(qoff, 1024),
                                             0.0, None, ALU.min), inc="tV")
                        v = p.cnt["tV"]
                        p.wait("scalar", "tV", v)
                        sv = p.op("scalar", lambda e, soff=soff: e.activation(
                            ub(soff, 1024), ub(soff, 1024), AF.Exp), inc="tC")
                        p.wait("vector", "tC", sv)
                        p.op("vector", lambda e, qoff=qoff: e.tensor_scalar(
                            ub(O_SQS, 1024), ub(qoff, 1024), 0.0, None,
                            ALU.max), inc="tV")
                        p.op("vector", lambda e, soff=soff:
                             e.scalar_tensor_tensor(
                                 ub(soff, 1024), ub(soff, 1024), -1.0,
                                 ub(O_SQS, 1024), ALU.add, ALU.add), inc="tV")
                    sig_done = p.cnt["tV"]
                    p.wait("tensor", "tV", sig_done)
                    for ts in range(2):
                        for kc in range(NKVCH):
                            p.op("tensor", lambda e, ts=ts, kc=kc: e.matmul(
                                pc[:, ts * 512:(ts + 1) * 512],
                                mncol[:, kc * 128:(kc + 1) * 128],
                                ub(O_SIG + kc * 1024 + ts * 512, 512),
                                start=(kc == 0), stop=(kc == NKVCH - 1)),
                                inc=("tT" if kc == NKVCH - 1 else None))
                        v = p.cnt["tT"]
                        p.wait("vector", "tT", v)
                        p.op("vector", lambda e, ts=ts: e.reciprocal(
                            scrB[:, ts * 512:(ts + 1) * 512],
                            pc[:, ts * 512:(ts + 1) * 512]), inc="tV")
                    den_done = p.cnt["tV"]
                    p.wait("tensor", "tV", den_done)
                    last_ev = [0, 0]
                    for fc in range(NKVCH):
                        for ts in range(2):
                            r = (fc * 2 + ts) % 2
                            ps = [pa, pb][r]
                            p.wait("tensor", "tC", last_ev[r])
                            for kc in range(NKVCH):
                                p.op("tensor", lambda e, fc=fc, ts=ts, kc=kc,
                                     ps=ps: e.matmul(
                                         ps[:, :512],
                                         memsb[:, kc * KV + fc * 128:
                                               kc * KV + (fc + 1) * 128],
                                         ub(O_SIG + kc * 1024 + ts * 512, 512),
                                         start=(kc == 0),
                                         stop=(kc == NKVCH - 1)),
                                     inc=("tT" if kc == NKVCH - 1 else None))
                            v = p.cnt["tT"]
                            p.wait("scalar", "tT", v)
                            CC = g * 4 + fc
                            doff = O_DELTA + CC * T + th * 1024 + ts * 512
                            tmp = O_SQS + r * 512
                            p.wait("scalar", "tV")
                            ac = p.op("scalar", lambda e, ps=ps, CC=CC, b=b,
                                      tmp=tmp: e.activation(
                                          ub(tmp, 512), ps[:, :512],
                                          AF.Identity,
                                          scale=gawc[:, CC * 2 + b:
                                                     CC * 2 + b + 1]),
                                      inc="tC")
                            last_ev[r] = ac
                            p.wait("vector", "tC", ac)
                            p.op("vector", lambda e, doff=doff, ts=ts, tmp=tmp:
                                 e.tensor_tensor(
                                     ub(doff, 512), ub(tmp, 512),
                                     scrB[:, ts * 512:(ts + 1) * 512],
                                     ALU.mult), inc="tV")
            amem_done = p.cnt["tV"]
            dump("amem", lambda: ub(O_DELTA, 16384))

            p.wait("tensor", "tV", amem_done)
            p.wait("tensor", "tG", vones_done)
            PBUF = {("A", 0): O_PA, ("A", 1): O_PA2,
                    ("B", 0): O_PB, ("B", 1): O_PB2}
            epil_v = 0
            epil_c = 0
            for c in range(NKV):
                for qh in range(2):
                    hA, hB = 2 * c, 2 * c + 1
                    exp_cnt = {}
                    pv_cnt = {}
                    for kt in range(16):
                        for head, hh, Sps, qb2 in (("A", hA, pa, 0),
                                                   ("B", hB, pb, 64)):
                            if kt > 0:
                                p.wait("tensor", "tC", exp_cnt[(head, kt - 1)])
                            elif (c, qh) != (0, 0):
                                p.wait("tensor", "tV", epil_v)
                                p.wait("tensor", "tC", epil_c)
                            for n in range(2):
                                p.op("tensor", lambda e, c=c, qh=qh, kt=kt,
                                     n=n, Sps=Sps, qb2=qb2: e.matmul(
                                         Sps[:, n * 512:(n + 1) * 512],
                                         ub(O_KSB + c * T + kt * 128, 128)
                                         [qb2:qb2 + 64, :],
                                         ub(O_QSB + c * T + qh * 1024 +
                                            n * 512, 512)[qb2:qb2 + 64, :],
                                         start=True, stop=True),
                                     inc=("tT" if n == 1 else None))
                            smm = p.cnt["tT"]
                            p.wait("scalar", "tT", smm)
                            if kt > 1:
                                p.wait("scalar", "tT", pv_cnt[(head, kt - 2)])
                            exp_cnt[(head, kt)] = p.op(
                                "scalar", lambda e, head=head, kt=kt, hh=hh,
                                b=b, Sps=Sps: e.activation(
                                    ub(PBUF[(head, kt % 2)], 1024),
                                    Sps[:, :1024], AF.Exp, scale=0.125,
                                    bias=negM[:, b * 16 + hh:b * 16 + hh + 1]),
                                inc="tC")
                        for head, hh, Op in (("A", hA, pc), ("B", hB, pd)):
                            p.wait("tensor", "tC", exp_cnt[(head, kt)])
                            lo = (64 + 128 * c) if hh % 2 == 0 else (128 * c)
                            for n in range(2):
                                p.op("tensor", lambda e, kt=kt, lo=lo, n=n,
                                     Op=Op, head=head: e.matmul(
                                         Op[:, n * 512:(n + 1) * 512],
                                         ub(O_VSB + kt * VW + lo, 128),
                                         ub(PBUF[(head, kt % 2)] + n * 512,
                                            512),
                                         start=(kt == 0), stop=(kt == 15)),
                                     inc=("tT" if n == 1 else None))
                            pv_cnt[(head, kt)] = p.cnt["tT"]
                    for head, hh, Op, Sps in (("A", hA, pc, pa),
                                              ("B", hB, pd, pb)):
                        even = (hh % 2 == 0)
                        drow = 64 if even else 0
                        obase = 0 if even else 64
                        p.wait("vector", "tT", pv_cnt[(head, 15)])
                        p.op("vector", lambda e, Op=Op, drow=drow:
                             recip_lp(e,
                                      ub(O_RROW, 1024)[drow:drow + 1, :],
                                      Op[drow:drow + 1, :1024]), inc="tV")
                        v = p.cnt["tV"]
                        p.wait("tensor", "tV", v)
                        for n in range(2):
                            p.op("tensor", lambda e, Sps=Sps, obase=obase,
                                 n=n, drow=drow: e.matmul(
                                     Sps[obase:obase + 64,
                                         n * 512:(n + 1) * 512],
                                     onesb[drow:drow + 1, :64],
                                     ub(O_RROW + n * 512, 512)
                                     [drow:drow + 1, :],
                                     start=True, stop=True,
                                     tile_position=(drow, obase)),
                                 inc=("tT" if n == 1 else None))
                        v = p.cnt["tT"]
                        p.wait("scalar", "tT", v)
                        gr = p.op("scalar", lambda e, Sps=Sps, obase=obase,
                                  c=c, b=b: e.activation(
                                      ub(O_S1, 1024)[obase:obase + 64, :],
                                      Sps[obase:obase + 64, :1024], AF.Identity,
                                      scale=g1wc[obase:obase + 64,
                                                 c * 2 + b:c * 2 + b + 1]),
                                  inc="tC")
                        p.wait("vector", "tC", gr)
                        p.op("vector", lambda e, Op=Op, obase=obase:
                             e.tensor_tensor(
                                 ub(O_RROW, 1024)[obase:obase + 64, :],
                                 Op[obase:obase + 64, :1024],
                                 ub(O_S1, 1024)[obase:obase + 64, :],
                                 ALU.mult), inc="tV")
                        doff = O_DELTA + c * T + qh * 1024
                        p.op("vector", lambda e, doff=doff, obase=obase, c=c,
                             b=b: e.scalar_tensor_tensor(
                                 ub(doff, 1024)[obase:obase + 64, :],
                                 ub(O_RROW, 1024)[obase:obase + 64, :],
                                 g1wvbc[obase:obase + 64,
                                        c * 2 + b:c * 2 + b + 1],
                                 ub(doff, 1024)[obase:obase + 64, :],
                                 ALU.add, ALU.add), inc="tV")
                    epil_v = p.cnt["tV"]
                    epil_c = p.cnt["tC"]
            attn_done = p.cnt["tV"]
            dump("delta", lambda: ub(O_DELTA, 16384))

            p.wait("tensor", "tV", attn_done)
            ev_prev = 0
            for tt in range(16):
                view = paB if tt % 2 == 0 else pbB
                if tt >= 2:
                    p.wait("tensor", "tV", ev_prev - 1)
                for cc in range(NCH):
                    p.op("tensor", lambda e, tt=tt, cc=cc, view=view:
                         e.transpose(view[:, cc * 128:(cc + 1) * 128],
                                     ub(O_DELTA + cc * T + tt * 128, 128),
                                     identb[:, :]),
                         inc=("tT" if cc == NCH - 1 else None))
                v = p.cnt["tT"]
                p.wait("vector", "tT", v)
                ev_prev = p.op("vector", lambda e, tt=tt, view=view:
                               e.tensor_copy(ub(O_D1TM + tt * 1024, 1024),
                                             view[:, :1024]), inc="tV")
            d1tm_done = p.cnt["tV"]

            half_fr = [0, 0]
            half_s2 = [0, 0]
            half_wr = [0, 0]
            p.wait("gpsimd", "tV", d1tm_done)
            for tt in range(16):
                half = tt % 2
                sx = "dXa" if half == 0 else "dXb"
                sy = "dY" if half == 0 else "dYb"
                p.wait("gpsimd", "tV", half_fr[half])
                p.wait("gpsimd", sy, half_wr[half])
                ld = p.op("gpsimd", lambda e, b=b, tt=tt, half=half:
                          e.dma_start(scrA[:, half * 1024:(half + 1) * 1024],
                                      x_d[b, tt * 128:(tt + 1) * 128, :]),
                          inc=sx, n=16)
                p.wait("vector", sx, ld)
                x1v = p.op("vector", lambda e, tt=tt, half=half:
                           e.tensor_tensor(
                               scrA[:, half * 1024:(half + 1) * 1024],
                               scrA[:, half * 1024:(half + 1) * 1024],
                               ub(O_D1TM + tt * 1024, 1024), ALU.add),
                           inc="tV")
                p.op("vector", lambda e, tt=tt, half=half: e.tensor_reduce(
                    statm[:, tt:tt + 1],
                    scrA[:, half * 1024:(half + 1) * 1024], AX.X, ALU.add),
                    inc="tV")
                p.wait("scalar", "tV", x1v)
                p.wait("scalar", "tV", half_s2[half])
                sq = p.op("scalar", lambda e, half=half: e.activation(
                    scrB[:, half * 1024:(half + 1) * 1024],
                    scrA[:, half * 1024:(half + 1) * 1024], AF.Square),
                    inc="tC")
                p.wait("vector", "tC", sq)
                p.op("vector", lambda e, tt=tt, half=half: e.tensor_reduce(
                    stat2[:, tt:tt + 1],
                    scrB[:, half * 1024:(half + 1) * 1024], AX.X, ALU.add),
                    inc="tV")
                half_s2[half] = p.cnt["tV"]
                half_fr[half] = p.cnt["tV"]
                p.wait("gpsimd", "tV", x1v)
                half_wr[half] = p.op("gpsimd", lambda e, b=b, tt=tt, half=half:
                                     e.dma_start(
                                         x1_sc[b, tt * 128:(tt + 1) * 128, :],
                                         scrA[:, half * 1024:
                                              (half + 1) * 1024]),
                                     inc=sy, n=16)
            x1_doneA = p.cnt["dY"]
            x1_doneB = p.cnt["dYb"]
            st2v = p.cnt["tV"]
            p.wait("vector", "tV", st2v)
            p.wait("vector", "dY", x1_doneA)
            p.wait("vector", "dYb", x1_doneB)
            p.wait("scalar", "tV", st2v)
            bar = p.op("scalar", lambda e: e.copy(scrB[:, 100:101],
                                                  statm[:, 0:1]), inc="tC")
            p.wait("vector", "tC", bar)
            p.op("vector", lambda e: e.tensor_scalar(scrB[:, 0:16],
                                                     statm[:, :], 1.0 / D,
                                                     None, ALU.mult), inc="tV")
            p.wait("vector", "tV")
            p.op("vector", lambda e: e.tensor_tensor(scrB[:, 16:32],
                                                     scrB[:, 0:16],
                                                     scrB[:, 0:16], ALU.mult),
                 inc="tV")
            p.wait("vector", "tV")
            p.op("vector", lambda e: e.scalar_tensor_tensor(
                scrB[:, 32:48], stat2[:, :], 1.0 / D, scrB[:, 16:32],
                ALU.mult, ALU.subtract), inc="tV")
            p.wait("vector", "tV")
            p.op("vector", lambda e: e.tensor_scalar(scrB[:, 48:64],
                                                     scrB[:, 32:48], EPS,
                                                     None, ALU.add), inc="tV")
            p.wait("vector", "tV")
            p.op("vector", lambda e: e.reciprocal(scrB[:, 64:80],
                                                  scrB[:, 48:64]), inc="tV")
            v = p.cnt["tV"]
            p.wait("scalar", "tV", v)
            p.op("scalar", lambda e: e.activation(rstdc[:, :], scrB[:, 64:80],
                                                  AF.Sqrt), inc="tC")
            v = p.cnt["tC"]
            p.wait("vector", "tC", v)
            p.op("vector", lambda e: e.scalar_tensor_tensor(
                negmrc[:, :], scrB[:, 0:16], -1.0, rstdc[:, :], ALU.mult,
                ALU.mult), inc="tV")
            rstd2_done = p.cnt["tV"]
            p.wait("scalar", "tV", rstd2_done)
            p.wait("gpsimd", "dY", x1_doneA)
            p.wait("gpsimd", "dYb", x1_doneB)
            half_ap = [0, 0]
            for tt in range(16):
                half = tt % 2
                sx = "dXa" if half == 0 else "dXb"
                p.wait("gpsimd", "tC", half_ap[half])
                ld = p.op("gpsimd", lambda e, b=b, tt=tt, half=half:
                          e.dma_start(scrA[:, half * 1024:(half + 1) * 1024],
                                      x1_sc[b, tt * 128:(tt + 1) * 128, :]),
                          inc=sx, n=16)
                p.wait("scalar", sx, ld)
                half_ap[half] = p.op("scalar", lambda e, tt=tt, half=half:
                                     e.activation(
                                         ub(O_LN2TM + tt * 1024, 1024),
                                         scrA[:, half * 1024:(half + 1) * 1024],
                                         AF.Identity, bias=negmrc[:, tt:tt + 1],
                                         scale=rstdc[:, tt:tt + 1]), inc="tC")
            ln2_done = p.cnt["tC"]
            p.wait("tensor", "tC", ln2_done)
            ev_prev = 0
            for cc in range(NCH):
                view = paB if cc % 2 == 0 else pbB
                if cc >= 2:
                    p.wait("tensor", "tV", ev_prev - 1)
                for tt in range(16):
                    p.op("tensor", lambda e, cc=cc, tt=tt, view=view:
                         e.transpose(view[:, tt * 128:(tt + 1) * 128],
                                     ub(O_LN2TM + tt * 1024 + cc * 128, 128),
                                     identb[:, :]),
                         inc=("tT" if tt == 15 else None))
                v = p.cnt["tT"]
                p.wait("vector", "tT", v)
                ev_prev = p.op("vector", lambda e, cc=cc, view=view:
                               e.tensor_copy(ub(O_LN2FM + cc * T, T),
                                             view[:, :T]), inc="tV")
                p.op("vector", lambda e, cc=cc, b=b: e.tensor_scalar(
                    scrB[:, 0:1024], ub(O_LN2FM + cc * T, 1024),
                    cbpc[:, cc * 2 + b:cc * 2 + b + 1],
                    pcols[:, (3 * NCH + cc) * 2 + b:(3 * NCH + cc) * 2 + b + 1],
                    ALU.mult, ALU.add), inc="tV")
                p.op("vector", lambda e: e.tensor_reduce(
                    qcol[:, 0:1], scrB[:, 0:1024], AX.X, ALU.max,
                    apply_absolute_value=True), inc="tV")
                p.wait("vector", "tV")
                p.op("vector", lambda e: e.tensor_tensor(
                    gmxcol[:, 0:1], gmxcol[:, 0:1], qcol[:, 0:1], ALU.max),
                    inc="tV")
                p.wait("vector", "tV")
                p.op("vector", lambda e, cc=cc, b=b: e.tensor_scalar(
                    scrB[:, 1024:2048], ub(O_LN2FM + cc * T + 1024, 1024),
                    cbpc[:, cc * 2 + b:cc * 2 + b + 1],
                    pcols[:, (3 * NCH + cc) * 2 + b:(3 * NCH + cc) * 2 + b + 1],
                    ALU.mult, ALU.add), inc="tV")
                p.op("vector", lambda e: e.tensor_reduce(
                    qcol[:, 0:1], scrB[:, 1024:2048], AX.X, ALU.max,
                    apply_absolute_value=True), inc="tV")
                p.wait("vector", "tV")
                mx = p.op("vector", lambda e: e.tensor_tensor(
                    gmxcol[:, 0:1], gmxcol[:, 0:1], qcol[:, 0:1], ALU.max),
                    inc="tV")
                p.wait("gpsimd", "tV", mx)
                p.op("gpsimd", lambda e, b=b, cc=cc: e.dma_start(
                    ln2_sc[b, cc, :, :], ub(O_LN2FM + cc * T, T)),
                    inc="dG", n=16)
            prev_ln2fm_done = p.cnt["tV"]

        ln2_wr_done = p.cnt["dG"]

        gm = p.cnt["tV"]
        p.wait("tensor", "tV", gm)
        p.op("tensor", lambda e: e.transpose(pc[:1, :128], gmxcol[:, :],
                                             identf[:, :]), inc="tT")
        v = p.cnt["tT"]
        p.wait("vector", "tT", v)
        p.op("vector", lambda e: e.tensor_reduce(srow[:1, 2:3], pc[:1, :128],
                                                 AX.X, ALU.max), inc="tV")
        p.wait("vector", "tV")
        v = p.cnt["tV"]
        p.wait("gpsimd", "tV", v)
        p.op("gpsimd", lambda e: e.dma_start(cc1i[:, :], srow[:1, 2:3]),
             inc="dG", n=16)
        v = p.cnt["dG"]
        p.wait("gpsimd", "dG", v)
        p.op("gpsimd", lambda e: e.collective_compute(
            "AllReduce", ALU.max, replica_groups=[list(range(NCORES))],
            ins=[cc1i[:, :]], outs=[cc1o[:, :]]), inc="cc")
        p.wait("gpsimd", "cc")
        p.op("gpsimd", lambda e: e.dma_start(srow[0:1, 2:3], cc1o[:, :]),
             inc="dG", n=16)
        v = p.cnt["dG"]
        p.wait("gpsimd", "dG", v)
        p.wait("scalar", "dG", v)
        p.op("scalar", lambda e: e.activation(srow[0:1, 3:4], srow[0:1, 10:11],
                                              AF.Sigmoid), inc="tC")
        v = p.cnt["tC"]
        p.wait("vector", "tC", v)
        p.op("vector", lambda e: e.tensor_tensor(srow[0:1, 4:5], srow[0:1, 3:4],
                                                 srow[0:1, 2:3], ALU.mult),
             inc="tV")
        p.wait("vector", "tV")
        p.op("vector", lambda e: e.tensor_scalar(srow[0:1, 5:6], srow[0:1, 4:5],
                                                 1.0 / 127.0, None, ALU.mult),
             inc="tV")
        p.wait("vector", "tV")
        p.op("vector", lambda e: e.reciprocal(srow[0:1, 4:5], srow[0:1, 4:5]),
             inc="tV")
        p.wait("vector", "tV")
        p.op("vector", lambda e: e.tensor_scalar(srow[0:1, 4:5], srow[0:1, 4:5],
                                                 127.0, None, ALU.mult),
             inc="tV")
        v = p.cnt["tV"]
        p.wait("gpsimd", "tV", v)
        p.op("gpsimd", lambda e: e.dma_start(rsxcol[0:1, 0:1], srow[0:1, 4:5]),
             inc="dG", n=16)
        p.op("gpsimd", lambda e: e.dma_start(sxcol[0:1, 0:1], srow[0:1, 5:6]),
             inc="dG", n=16)
        v = p.cnt["dG"]
        p.wait("gpsimd", "dG", v)
        for col in (rsxcol, sxcol):
            for step in (1, 2, 4, 8, 16, 32, 64):
                vv = p.op("gpsimd", lambda e, col=col, step=step: e.dma_start(
                    col[step:2 * step, 0:1], col[0:step, 0:1]), inc="dG", n=16)
                p.wait("gpsimd", "dG", vv)
        sx_done = p.cnt["dG"]

        p.wait("vector", "dG", sx_done)
        kx_half = [0, 0]
        for b in range(B):
            for cc in range(NCH):
                half = (b * NCH + cc) % 2
                sx = "dXa" if half == 0 else "dXb"
                p.wait("gpsimd", "tV", kx_half[half])
                ld = p.op("gpsimd", lambda e, b=b, cc=cc, half=half:
                          e.dma_start(ub(O_LN2LD + half * 2048, 2048),
                                      ln2_sc[b, cc, :, :]), inc=sx, n=16)
                p.wait("vector", sx, ld)
                p.op("vector", lambda e, cc=cc, b=b, half=half: e.tensor_scalar(
                    scrB[:, :], ub(O_LN2LD + half * 2048, 2048),
                    cbpc[:, cc * 2 + b:cc * 2 + b + 1],
                    pcols[:, (3 * NCH + cc) * 2 + b:(3 * NCH + cc) * 2 + b + 1],
                    ALU.mult, ALU.add), inc="tV")
                kx_half[half] = p.cnt["tV"]
                p.op("vector", lambda e: e.tensor_scalar(
                    scrA[:, :], scrB[:, :], rsxcol[:, 0:1], MAGIC, ALU.mult,
                    ALU.add), inc="tV")
                p.op("vector", lambda e: e.tensor_scalar(
                    scrA[:, :], scrA[:, :], -MAGIC, 127.0, ALU.add, ALU.min),
                    inc="tV")
                p.op("vector", lambda e, b=b, cc=cc: e.tensor_scalar(
                    ub(O_KX + cc * 4096 + b * 2048, 2048), scrA[:, :], -128.0,
                    None, ALU.max), inc="tV")
        kx_done = p.cnt["tV"]
        dump("kx", lambda: ub(O_KX, 32768))

        wd_xbar = 0
        for hc in range(NCH):
            p.wait("gpsimd", "dWd", wd_xbar)
            ld = p.op("gpsimd", lambda e, hc=hc: e.dma_start(
                uf(O_WDRAW, 2816), wd_d[hc * 128:(hc + 1) * 128, :]),
                inc="dW", n=16)
            p.wait("vector", "dW", ld)
            p.op("vector", lambda e, hc=hc: e.tensor_reduce(
                rsdwn[:, hc:hc + 1], uf(O_WDRAW, 2816), AX.X, ALU.max,
                apply_absolute_value=True), inc="tV")
            p.wait("vector", "tV")
            p.op("vector", lambda e, hc=hc: e.reciprocal(qcol[:, 0:1],
                                                         rsdwn[:, hc:hc + 1]),
                 inc="tV")
            p.wait("vector", "tV")
            p.op("vector", lambda e: e.tensor_scalar(qcol[:, 0:1],
                                                     qcol[:, 0:1], 127.0,
                                                     None, ALU.mult), inc="tV")
            p.wait("vector", "tV")
            p.op("vector", lambda e, hc=hc: e.tensor_scalar(
                rsdwn[:, hc:hc + 1], rsdwn[:, hc:hc + 1], 1.0 / 127.0, None,
                ALU.mult), inc="tV")
            p.wait("vector", "tV")
            p.op("vector", lambda e: e.tensor_scalar(
                uf(O_WDRAW, 2816), uf(O_WDRAW, 2816), qcol[:, 0:1], MAGIC,
                ALU.mult, ALU.add), inc="tV")
            p.op("vector", lambda e: e.tensor_scalar(
                uf(O_WDRAW, 2816), uf(O_WDRAW, 2816), -MAGIC, 127.0, ALU.add,
                ALU.min), inc="tV")
            p.op("vector", lambda e: e.tensor_scalar(
                ub(O_WDNAT, 2816), uf(O_WDRAW, 2816), -128.0, None, ALU.max),
                inc="tV")
            v = p.cnt["tV"]
            p.wait("sync", "tV", v)
            for ic in range(NIC):
                p.op("sync", lambda e, ic=ic, hc=hc: e.dma_start_transpose(
                    ub(O_KWDT + ic * 1024 + hc * 128, 128),
                    ub(O_WDNAT + ic * 128, 128)), inc="dWd", n=16)
            wd_xbar = p.cnt["dWd"]
        kwdt_done = p.cnt["dWd"]

        p.wait("gpsimd", "dW")
        p.wait("tensor", "tV", kx_done)
        ic_mm_done = {}
        ic_quant = {}
        hm_slot = [0, 0, 0, 0]
        gu_evac = [0, 0]
        gu_evac_v = [0, 0]
        silu_end = {}
        for ic in range(NIC):
            if ic >= 1:
                p.wait("gpsimd", "tV", ic_quant[ic - 1])
            for w_i, wsrc in ((0, wg_d), (1, wu_d)):
                p.op("gpsimd", lambda e, ic=ic, w_i=w_i, wsrc=wsrc:
                     e.dma_start(uf(O_RAW + w_i * 4096, 1024),
                                 wsrc[ic * 128:(ic + 1) * 128, :]),
                     inc="dW", n=16)
            rawv = p.cnt["dW"]
            p.wait("vector", "dW", rawv)
            if ic >= 1:
                sxp = "dKa" if (ic - 1) % 2 == 0 else "dKb"
                p.wait("vector", sxp)
            if ic >= 2:
                p.wait("vector", "tC", silu_end.get(ic - 2, 0))
            sl = ic % 2
            for w_i, rscol, rcol in ((0, rsgc, None), (1, None, rsuall)):
                p.op("vector", lambda e, w_i=w_i: e.tensor_reduce(
                    qcol[:, 0:1], uf(O_RAW + w_i * 4096, 1024), AX.X, ALU.max,
                    apply_absolute_value=True), inc="tV")
                p.wait("vector", "tV")
                if w_i == 0:
                    p.op("vector", lambda e, sl=sl:
                         e.scalar_tensor_tensor(
                             rsgc[:, sl:sl + 1], qcol[:, 0:1], 1.0 / 127.0,
                             sxcol[:, 0:1], ALU.mult, ALU.mult), inc="tV")
                else:
                    p.op("vector", lambda e, ic=ic:
                         e.scalar_tensor_tensor(
                             rsuall[:, ic:ic + 1], qcol[:, 0:1], 1.0 / 127.0,
                             sxcol[:, 0:1], ALU.mult, ALU.mult), inc="tV")
                p.wait("vector", "tV")
                p.op("vector", lambda e: e.reciprocal(qcol[:, 0:1],
                                                      qcol[:, 0:1]), inc="tV")
                p.wait("vector", "tV")
                p.op("vector", lambda e: e.tensor_scalar(
                    qcol[:, 0:1], qcol[:, 0:1], 127.0, None, ALU.mult),
                    inc="tV")
                p.wait("vector", "tV")
                p.op("vector", lambda e, w_i=w_i: e.tensor_scalar(
                    uf(O_RAW + w_i * 4096, 1024), uf(O_RAW + w_i * 4096, 1024),
                    qcol[:, 0:1], MAGIC, ALU.mult, ALU.add), inc="tV")
                p.op("vector", lambda e, w_i=w_i: e.tensor_scalar(
                    uf(O_RAW + w_i * 4096, 1024), uf(O_RAW + w_i * 4096, 1024),
                    -MAGIC, 127.0, ALU.add, ALU.min), inc="tV")
                p.op("vector", lambda e, w_i=w_i: e.tensor_scalar(
                    ub(O_KWNAT + w_i * 1024, 1024),
                    uf(O_RAW + w_i * 4096, 1024), -128.0, None, ALU.max),
                    inc="tV")
            ic_quant[ic] = p.cnt["tV"]
            p.wait("sync", "tV", ic_quant[ic])
            if ic >= 2:
                p.wait("sync", "tT", ic_mm_done.get(ic - 2, 0))
            kslot = O_KWT + sl * 2048
            ksem = "dKa" if sl == 0 else "dKb"
            for w_i in range(2):
                for hc in range(NCH):
                    p.op("sync", lambda e, w_i=w_i, hc=hc, kslot=kslot:
                         e.dma_start_transpose(
                             ub(kslot + w_i * 1024 + hc * 128, 128),
                             ub(O_KWNAT + w_i * 1024 + hc * 128, 128)),
                         inc=ksem, n=16)
            trv = p.cnt[ksem]
            p.wait("tensor", ksem, trv)
            for ts in range(8):
                par = ts % 2
                gps = pa if par == 0 else pb
                ups = pc if par == 0 else pd
                p.wait("tensor", "tC", gu_evac[par])
                p.wait("tensor", "tV", gu_evac_v[par])
                for hc in range(NCH):
                    p.op("tensor", lambda e, ts=ts, hc=hc, kslot=kslot,
                         gps=gps: e.matmul(
                             gps[:, :512], ub(kslot + hc * 128, 128),
                             ub(O_KX + hc * 4096 + ts * 512, 512),
                             start=(hc == 0), stop=(hc == NCH - 1)),
                         inc=("tT" if hc == NCH - 1 else None))
                gv = p.cnt["tT"]
                for hc in range(NCH):
                    p.op("tensor", lambda e, ts=ts, hc=hc, kslot=kslot,
                         ups=ups: e.matmul(
                             ups[:, :512], ub(kslot + 1024 + hc * 128, 128),
                             ub(O_KX + hc * 4096 + ts * 512, 512),
                             start=(hc == 0), stop=(hc == NCH - 1)),
                         inc=("tT" if hc == NCH - 1 else None))
                uv = p.cnt["tT"]
                p.wait("scalar", "tT", gv)
                p.wait("scalar", "tV", gu_evac_v[0])
                p.wait("scalar", "tV", gu_evac_v[1])
                sv = p.op("scalar", lambda e, gps=gps, sl=sl: e.activation(
                    ub(O_HMIDB + 3584, 512), gps[:, :512], AF.Silu,
                    scale=rsgc[:, sl:sl + 1]), inc="tC")
                gu_evac[par] = sv
                hslot = ts % 4
                p.wait("vector", "tC", sv)
                p.wait("vector", "tT", uv)
                p.wait("vector", "dH%d" % hslot, hm_slot[hslot])
                hv = p.op("vector", lambda e, ups=ups, hslot=hslot:
                          e.tensor_tensor(
                              ub(O_HMIDB + hslot * 512, 512), ups[:, :512],
                              ub(O_HMIDB + 3584, 512), ALU.mult), inc="tV")
                gu_evac_v[par] = hv
                p.op("vector", lambda e, hslot=hslot: e.tensor_reduce(
                    qcol[:, 0:1], ub(O_HMIDB + hslot * 512, 512), AX.X,
                    ALU.max, apply_absolute_value=True), inc="tV")
                p.wait("vector", "tV")
                p.op("vector", lambda e, ic=ic: e.tensor_tensor(
                    qcol2[:, 0:1], qcol[:, 0:1], rsuall[:, ic:ic + 1],
                    ALU.mult), inc="tV")
                p.wait("vector", "tV")
                p.op("vector", lambda e: e.tensor_tensor(
                    hmxcol[:, 0:1], hmxcol[:, 0:1], qcol2[:, 0:1], ALU.max),
                    inc="tV")
                p.wait("gpsimd", "tV", hv)
                hm_slot[hslot] = p.op("gpsimd", lambda e, ic=ic, ts=ts,
                                      hslot=hslot: e.dma_start(
                                          hmid_sc[ic, :,
                                                  ts * 512:(ts + 1) * 512],
                                          ub(O_HMIDB + hslot * 512, 512)),
                                      inc="dH%d" % hslot, n=16)
            ic_mm_done[ic] = p.cnt["tT"]
            silu_end[ic] = p.cnt["tC"]
        gu_done = p.cnt["tV"]

        p.wait("tensor", "tV", gu_done)
        p.op("tensor", lambda e: e.transpose(pc[:1, :128], hmxcol[:, :],
                                             identf[:, :]), inc="tT")
        v = p.cnt["tT"]
        p.wait("vector", "tT", v)
        p.op("vector", lambda e: e.tensor_reduce(srow[:1, 6:7], pc[:1, :128],
                                                 AX.X, ALU.max), inc="tV")
        p.wait("vector", "tV")
        v = p.cnt["tV"]
        p.wait("gpsimd", "tV", v)
        p.op("gpsimd", lambda e: e.dma_start(cc2i[:, :], srow[:1, 6:7]),
             inc="dG", n=16)
        v = p.cnt["dG"]
        p.wait("gpsimd", "dG", v)
        p.op("gpsimd", lambda e: e.collective_compute(
            "AllReduce", ALU.max, replica_groups=[list(range(NCORES))],
            ins=[cc2i[:, :]], outs=[cc2o[:, :]]), inc="cc")
        p.wait("gpsimd", "cc")
        p.op("gpsimd", lambda e: e.dma_start(srow[0:1, 6:7], cc2o[:, :]),
             inc="dG", n=16)
        v = p.cnt["dG"]
        p.wait("gpsimd", "dG", v)
        p.wait("scalar", "dG", v)
        p.op("scalar", lambda e: e.activation(srow[0:1, 7:8], srow[0:1, 11:12],
                                              AF.Sigmoid), inc="tC")
        v = p.cnt["tC"]
        p.wait("vector", "tC", v)
        p.op("vector", lambda e: e.tensor_tensor(srow[0:1, 8:9], srow[0:1, 7:8],
                                                 srow[0:1, 6:7], ALU.mult),
             inc="tV")
        p.wait("vector", "tV")
        p.op("vector", lambda e: e.tensor_scalar(srow[0:1, 9:10],
                                                 srow[0:1, 8:9], 1.0 / 127.0,
                                                 None, ALU.mult), inc="tV")
        p.wait("vector", "tV")
        p.op("vector", lambda e: e.reciprocal(srow[0:1, 8:9], srow[0:1, 8:9]),
             inc="tV")
        p.wait("vector", "tV")
        p.op("vector", lambda e: e.tensor_scalar(srow[0:1, 8:9],
                                                 srow[0:1, 8:9], 127.0, None,
                                                 ALU.mult), inc="tV")
        v = p.cnt["tV"]
        p.wait("gpsimd", "tV", v)
        p.op("gpsimd", lambda e: e.dma_start(rsdcol[0:1, 0:1], srow[0:1, 8:9]),
             inc="dG", n=16)
        p.op("gpsimd", lambda e: e.dma_start(sdcol[0:1, 0:1], srow[0:1, 9:10]),
             inc="dG", n=16)
        v = p.cnt["dG"]
        p.wait("gpsimd", "dG", v)
        for col in (rsdcol, sdcol):
            for step in (1, 2, 4, 8, 16, 32, 64):
                vv = p.op("gpsimd", lambda e, col=col, step=step: e.dma_start(
                    col[step:2 * step, 0:1], col[0:step, 0:1]), inc="dG", n=16)
                p.wait("gpsimd", "dG", vv)
        sd_done = p.cnt["dG"]
        p.wait("vector", "dG", sd_done)
        p.op("vector", lambda e: e.tensor_scalar(rsdwn[:, :], rsdwn[:, :],
                                                 sdcol[:, 0:1], None,
                                                 ALU.mult), inc="tV")
        p.wait("vector", "tV")
        for hc in range(NCH):
            for b in range(B):
                p.op("vector", lambda e, hc=hc, b=b: e.tensor_tensor(
                    combo[:, hc * 2 + b:hc * 2 + b + 1], rsdwn[:, hc:hc + 1],
                    gbc[:, hc * 2 + b:hc * 2 + b + 1], ALU.mult), inc="tV")
        for ic in range(NIC):
            p.op("vector", lambda e, ic=ic: e.tensor_tensor(
                rsq[:, ic:ic + 1], rsuall[:, ic:ic + 1], rsdcol[:, 0:1],
                ALU.mult), inc="tV")
        combo_done = p.cnt["tV"]

        p.wait("tensor", "dWd", kwdt_done)
        p.wait("tensor", "tV", combo_done)
        kh_slot_free = [0, 0]
        ld_half = [0, 0]
        out_half = [0, 0]
        x1_half = [0, 0]
        out_cnt = 0
        for ts in range(8):
            b = ts // 4
            slot = (ts % 2) * 11264
            p.wait("gpsimd", "tT", kh_slot_free[ts % 2])
            for ic in range(NIC):
                half = ic % 2
                sx = "dXa" if half == 0 else "dXb"
                p.wait("gpsimd", "tV", ld_half[half])
                ld = p.op("gpsimd", lambda e, ic=ic, ts=ts, half=half:
                          e.dma_start(ub(O_HMLD + half * 512, 512),
                                      hmid_sc[ic, :, ts * 512:(ts + 1) * 512]),
                          inc=sx, n=16)
                p.wait("vector", sx, ld)
                q1 = p.op("vector", lambda e, half=half, ic=ic: e.tensor_scalar(
                    scrA[:, 0:512], ub(O_HMLD + half * 512, 512),
                    rsq[:, ic:ic + 1], MAGIC, ALU.mult, ALU.add), inc="tV")
                ld_half[half] = q1
                p.op("vector", lambda e: e.tensor_scalar(
                    scrA[:, 0:512], scrA[:, 0:512], -MAGIC, 127.0, ALU.add,
                    ALU.min), inc="tV")
                p.op("vector", lambda e, ic=ic, slot=slot: e.tensor_scalar(
                    ub(O_KH + slot + ic * 512, 512), scrA[:, 0:512], -128.0,
                    None, ALU.max), inc="tV")
            khv = p.cnt["tV"]
            p.wait("tensor", "tV", khv)
            for ic in range(NIC):
                for hc in range(NCH):
                    ps = PS[hc // 2]
                    co = (hc % 2) * 512
                    p.op("tensor", lambda e, ic=ic, hc=hc, ps=ps, co=co,
                         slot=slot: e.matmul(
                             ps[:, co:co + 512],
                             ub(O_KWDT + ic * 1024 + hc * 128, 128),
                             ub(O_KH + slot + ic * 512, 512),
                             start=(ic == 0), stop=(ic == NIC - 1)),
                         inc=("tT" if (ic == NIC - 1 and hc == NCH - 1)
                              else None))
            dv = p.cnt["tT"]
            kh_slot_free[ts % 2] = dv
            p.wait("scalar", "tT", dv)
            for hc in range(NCH):
                ps = PS[hc // 2]
                co = (hc % 2) * 512
                p.op("scalar", lambda e, hc=hc, b=b, ps=ps, co=co:
                     e.activation(ub(O_D2B + hc * 512, 512), ps[:, co:co + 512],
                                  AF.Identity,
                                  scale=combo[:, hc * 2 + b:hc * 2 + b + 1]),
                     inc="tC")
            ev = p.cnt["tC"]
            p.wait("sync", "tC", ev)
            for sub in range(4):
                for hc in range(NCH):
                    p.op("sync", lambda e, sub=sub, hc=hc:
                         e.dma_start_transpose(
                             ub(O_D2TM + sub * 1024 + hc * 128, 128),
                             ub(O_D2B + hc * 512 + sub * 128, 128)),
                         inc="dD2", n=16)
            trv = p.cnt["dD2"]
            for sub in range(4):
                trow = (ts % 4) * 512 + sub * 128
                half = sub % 2
                osem = "dOa" if half == 0 else "dOb"
                sy = "dY" if half == 0 else "dYb"
                p.wait("gpsimd", "tV", x1_half[half])
                ld = p.op("gpsimd", lambda e, b=b, trow=trow, half=half:
                          e.dma_start(uf(O_X1LD + half * 2048, 1024),
                                      x1_sc[b, trow:trow + 128, :]),
                          inc=sy, n=16)
                p.wait("vector", sy, ld)
                p.wait("vector", "dD2", trv)
                p.wait("vector", osem, out_half[half])
                mv = p.op("vector", lambda e, sub=sub, half=half:
                          e.tensor_tensor(uf(O_OUTST + half * 2048, 1024),
                                          uf(O_X1LD + half * 2048, 1024),
                                          ub(O_D2TM + sub * 1024, 1024),
                                          ALU.add), inc="tV")
                x1_half[half] = mv
                p.wait("sync", "tV", mv)
                out_cnt = p.op("sync", lambda e, b=b, trow=trow, half=half:
                               e.dma_start(y_d[b, trow:trow + 128, :],
                                           uf(O_OUTST + half * 2048, 1024)),
                               inc=osem, n=16)
                out_half[half] = out_cnt
        p.wait("sync", "dOa")
        p.wait("sync", "dOb")
        p.wait("gpsimd", "dG")
        p.wait("gpsimd", "dY")
        p.wait("gpsimd", "dYb")

        with nc.Block() as block:
            @block.tensor
            def _(e):
                for fn in p.ops["tensor"]:
                    fn(e)

            @block.vector
            def _(e):
                for fn in p.ops["vector"]:
                    fn(e)

            @block.scalar
            def _(e):
                for fn in p.ops["scalar"]:
                    fn(e)

            @block.sync
            def _(e):
                for fn in p.ops["sync"]:
                    fn(e)

            @block.gpsimd
            def _(e):
                for fn in p.ops["gpsimd"]:
                    fn(e)

    return nc


def kernel(L0_attn_x, L0_attn_Wq, L0_attn_Wk, L0_attn_Wv, L0_attn_memory,
           L0_attn_memory_norm, L0_attn_memory_weight, L0_ffn_clip_x,
           L0_ffn_clip_down, L0_ffn_w_gate, L0_ffn_w_up, L0_ffn_w_down, L0_c,
           L0_wmod):
    x = np.ascontiguousarray(np.asarray(L0_attn_x, dtype=np.float32))
    nc = build()
    common = {
        "wq": np.ascontiguousarray(np.asarray(L0_attn_Wq, np.float32)),
        "wk": np.ascontiguousarray(np.asarray(L0_attn_Wk, np.float32)),
        "wv": np.ascontiguousarray(np.asarray(L0_attn_Wv, np.float32)),
        "mem": np.ascontiguousarray(np.asarray(L0_attn_memory, np.float32)),
        "mn": np.asarray(L0_attn_memory_norm, np.float32).reshape(1, KV),
        "mw": np.asarray(L0_attn_memory_weight, np.float32).reshape(1, 1),
        "clx": np.asarray(L0_ffn_clip_x, np.float32).reshape(1, 1),
        "cld": np.asarray(L0_ffn_clip_down, np.float32).reshape(1, 1),
        "wg": np.ascontiguousarray(np.asarray(L0_ffn_w_gate, np.float32)),
        "wu": np.ascontiguousarray(np.asarray(L0_ffn_w_up, np.float32)),
        "wd": np.ascontiguousarray(np.asarray(L0_ffn_w_down, np.float32)),
        "wmod": np.ascontiguousarray(np.asarray(L0_wmod, np.float32)),
    }
    cfull = np.ascontiguousarray(np.asarray(L0_c, np.float32))
    in_maps = []
    for core in range(NCORES):
        m = dict(common)
        m["x"] = np.ascontiguousarray(x[2 * core:2 * core + 2])
        m["cvec"] = np.ascontiguousarray(cfull[2 * core:2 * core + 2])
        in_maps.append(m)
    res = run_bass_kernel_spmd(nc, in_maps, core_ids=list(range(NCORES)))
    return np.concatenate([r["y"] for r in res.results], axis=0)
